# version 01
# baseline (speedup 1.0000x reference)
import numpy as np
import ml_dtypes
import concourse.bass as bass
import concourse.mybir as mybir
from concourse.bass_utils import run_bass_kernel_spmd
from contextlib import ExitStack

F32 = mybir.dt.float32
BF16 = mybir.dt.bfloat16
I8 = mybir.dt.int8
I32 = mybir.dt.int32
AF = mybir.ActivationFunctionType
OP = mybir.AluOpType
AX = mybir.AxisListType.X

B, T, C = 2, 8192, 1024
NCORE = 8
TC = 2048
NCH = C // 128
HID0 = 4096
NH0 = HID0 // 128
IFF = 2816
NIF = IFF // 128
HID1 = 4096
NH1 = HID1 // 128
SL = 512
EPS = 1e-5

ENGINES = ("sy", "gp", "v", "s", "t")


def _flat(deps):
    out = []
    for d in deps:
        if d is None:
            continue
        if isinstance(d, list):
            out.extend(x for x in d if x is not None)
        else:
            out.append(d)
    return out


class Prog:

    NLANES = 8

    def __init__(self, nc, stack):
        self.nc = nc
        self.stack = stack
        keys = list(ENGINES) + ["sy%d" % i for i in range(self.NLANES)]
        self.semkeys = keys
        self.sem = {e: stack.enter_context(nc.semaphore("sem_" + e))
                    for e in keys}
        self.cnt = {e: 0 for e in keys}
        self.ops = {e: [] for e in ENGINES}
        self.watermark = {e: {e2: 0 for e2 in keys} for e in ENGINES}
        self.fenced = set()
        self.rr = 0
        self._sb = {}

    def sbuf(self, name, shape, dtype, **kw):
        t = self.stack.enter_context(self.nc.sbuf_tensor(name, shape, dtype, **kw))
        self._sb[name] = t
        return t

    def psum(self, name, shape, dtype=F32):
        return self.stack.enter_context(self.nc.psum_tensor(name, shape, dtype))

    def emit(self, e, fn, deps=(), sig=True, fence=None):
        if fence is None:
            fence = e in ("v", "s", "g")
        need = {}
        for (de, dv) in _flat(deps):
            if de == e and e not in ("sy", "gp") and (de, dv) not in self.fenced:
                continue
            if dv > self.watermark[e][de]:
                need[de] = max(need.get(de, 0), dv)
        for de, dv in need.items():
            self.watermark[e][de] = dv
        inc = 16 if (sig and e in ("sy", "gp") and sig != "c") else 1
        if sig:
            self.cnt[e] += inc
        val = self.cnt[e]
        sem = self.sem
        me = self

        use_drain = sig and e in ("v", "s", "g")

        def cmd(eng):
            for de, dv in need.items():
                eng.wait_ge(sem[de], dv)
            r = fn(eng)
            if sig:
                if use_drain:
                    eng.drain().then_inc(sem[e], inc)
                else:
                    r.then_inc(sem[e], inc)
        self.ops[e].append(cmd)
        if fence:
            self.fenced.add((e, val))
        return (e, val)

    def dma(self, e, out, in_, deps=(), **kw):
        assert e == "sy"
        lane = "sy%d" % (self.rr % self.NLANES)
        self.rr += 1
        prev = self.cnt[lane]
        need = {}
        for (de, dv) in _flat(deps):
            if dv > self.watermark[e][de]:
                need[de] = max(need.get(de, 0), dv)
        for de, dv in need.items():
            self.watermark[e][de] = dv
        self.cnt[lane] += 16
        val = self.cnt[lane]
        sem = self.sem
        if prev > self.watermark[e][lane]:
            self.watermark[e][lane] = prev

        def cmd(eng):
            for de, dv in need.items():
                eng.wait_ge(sem[de], dv)
            if prev > 0:
                eng.wait_ge(sem[lane], prev)
            eng.dma_start(out=out, in_=in_, **kw).then_inc(sem[lane], 16)
        self.ops[e].append(cmd)
        return (lane, val)

    def gap(self, e, deps):
        need = {}
        for (de, dv) in _flat(deps):
            if de == e and e not in ("sy", "gp") and (de, dv) not in self.fenced:
                continue
            if dv > self.watermark[e][de]:
                need[de] = max(need.get(de, 0), dv)
        for de, dv in need.items():
            self.watermark[e][de] = dv
        sem = self.sem

        def cmd(eng):
            for de, dv in need.items():
                eng.wait_ge(sem[de], dv)
        self.ops[e].append(cmd)

    def token(self, e):
        if e == "sy":
            return [("sy%d" % i, self.cnt["sy%d" % i])
                    for i in range(self.NLANES)]
        return (e, self.cnt[e])

    def all_tokens(self):
        toks = []
        for e in ENGINES:
            t = self.token(e)
            if isinstance(t, list):
                toks += t
            else:
                toks.append(t)
        return toks

    def finalize(self):
        nc = self.nc
        with nc.Block() as block:
            eng_map = {}

            @block.sync
            def _(eng):
                for cmd in self.ops["sy"]:
                    cmd(eng)

            @block.gpsimd
            def _(eng):
                for cmd in self.ops["gp"]:
                    cmd(eng)

            @block.vector
            def _(eng):
                for cmd in self.ops["v"]:
                    cmd(eng)

            @block.scalar
            def _(eng):
                for cmd in self.ops["s"]:
                    cmd(eng)

            @block.tensor
            def _(eng):
                for cmd in self.ops["t"]:
                    cmd(eng)



def cast_load(p, dst_bf, src_f32_dram, stage_f32, deps=()):
    tl = p.dma("sy", stage_f32, src_f32_dram, deps=deps)
    return p.emit("v", lambda e: e.tensor_copy(dst_bf, stage_f32), deps=[tl])


def ceil_div(a, b):
    return (a + b - 1) // b



class LNKit:

    def __init__(self, p, kit):
        self.p = p
        self.k = kit


def build_common(p):
    nc = p.nc
    k = {}
    k["ones_col_f"] = p.sbuf("ones_col_f", [128, 1], F32)
    k["ones_col_b"] = p.sbuf("ones_col_b", [128, 1], BF16)
    k["ones_row_f"] = p.sbuf("ones_row_f", [1, SL], F32)
    k["ones_row_b"] = p.sbuf("ones_row_b", [1, SL], BF16)
    t1 = p.emit("v", lambda e: e.memset(k["ones_col_f"][:], 1.0))
    t2 = p.emit("v", lambda e: e.memset(k["ones_col_b"][:], 1.0))
    t3 = p.emit("v", lambda e: e.memset(k["ones_row_f"][:], 1.0))
    t4 = p.emit("v", lambda e: e.memset(k["ones_row_b"][:], 1.0))
    k["ones_tok"] = [t1, t2, t3, t4]
    k["rows"] = p.sbuf("rows", [1, 6 * SL], F32)
    k["tmp"] = p.sbuf("k_tmp", [128, SL], F32)
    return k


def emit_ln(p, k, src, nch, ps_pool, deps, sq_buf, out_fn,
            g_col=None, b_col=None, dst=None, w=SL):
    rows = k["rows"]
    onesf = k["ones_col_f"]
    onesb = k["ones_col_b"]
    st1 = ps_pool["stat1"][0:1, 0:w]
    st2 = ps_pool["stat2"][0:1, 0:w]
    sq = sq_buf
    mm_toks = []
    sq_toks = []
    for c in range(nch):
        a = src(c)
        tsq = p.emit("s", lambda e, a=a: e.activation(sq[:, 0:w], a, AF.Square),
                     deps=list(deps) + (sq_toks[-1:] if sq_toks else []))
        ones = onesf if a.dtype == F32 else onesb
        t_a = p.emit("t", lambda e, a=a, c=c, ones=ones: e.matmul(
            st1, ones[:], a, start=(c == 0), stop=(c == nch - 1)),
            deps=list(deps))
        t_b = p.emit("t", lambda e, c=c: e.matmul(
            st2, onesf[:], sq[:, 0:w], start=(c == 0), stop=(c == nch - 1)),
            deps=[tsq])
        sq_toks = [t_b]
        mm_toks = [t_a, t_b]
    ncf = float(nch * 128)
    rM = rows[0:1, 0:w]
    rQ = rows[0:1, SL:SL + w]
    rRS = rows[0:1, 2 * SL:2 * SL + w]
    rT = rows[0:1, 3 * SL:3 * SL + w]
    rMRS = rows[0:1, 4 * SL:4 * SL + w]
    t1 = p.emit("v", lambda e: e.tensor_scalar(rM, st1, -1.0 / ncf, None, OP.mult),
                deps=mm_toks)
    t2 = p.emit("v", lambda e: e.tensor_copy(rQ, st2), deps=mm_toks)
    t3 = p.emit("v", lambda e: e.tensor_mul(rT, rM, rM), deps=[t1])
    t4 = p.emit("v", lambda e: e.scalar_tensor_tensor(
        rQ, rQ, 1.0 / ncf, rT, OP.mult, OP.subtract), deps=[t2, t3])
    t4b = p.emit("v", lambda e: e.tensor_scalar(rQ, rQ, EPS, None, OP.add),
                 deps=[t4])
    t5 = p.emit("s", lambda e: e.activation(rT, rQ, AF.Sqrt), deps=[t4b])
    t6 = p.emit("v", lambda e: e.reciprocal(rRS, rT), deps=[t5], fence=True)
    t7 = p.emit("v", lambda e: e.tensor_mul(rMRS, rM, rRS), deps=[t6, t1])
    psR = ps_pool["outerR"][:, 0:w]
    psB = ps_pool["outerB"][:, 0:w]
    tR = p.emit("t", lambda e: e.matmul(psR, k["ones_row_f"][0:1, 0:128], rRS,
                                        start=True, stop=True), deps=[t6])
    tB = p.emit("t", lambda e: e.matmul(psB, k["ones_row_f"][0:1, 0:128], rMRS,
                                        start=True, stop=True), deps=[t7])
    out_toks = []
    tmp = k["tmp"]
    for c in range(nch):
        tv1 = p.emit("v", lambda e, c=c: e.tensor_mul(tmp[:, 0:w], src(c), psR),
                     deps=[tR, tB])
        if g_col is None:
            tok = p.emit("v", lambda e, c=c: e.tensor_add(
                dst(c), tmp[:, 0:w], psB), deps=[tv1])
        else:
            tv2 = p.emit("v", lambda e, c=c: e.tensor_add(
                tmp[:, 0:w], tmp[:, 0:w], psB), deps=[tv1])
            tok = p.emit("v", lambda e, c=c: e.tensor_scalar(
                dst(c), tmp[:, 0:w], g_col[:, c:c + 1], b_col[:, c:c + 1],
                OP.mult, OP.add), deps=[tv2])
        if out_fn is not None:
            tok = out_fn(c, tok)
        out_toks.append(tok)
    return out_toks



def build_launch_A():
    nc = bass.Bass("TRN2", target_bir_lowering=False, detect_race_conditions=False)
    xT = nc.dram_tensor("xT", [C, TC], F32, kind="ExternalInput").ap()
    projT = nc.dram_tensor("projT", [C, C], F32, kind="ExternalInput").ap()
    fc1T = nc.dram_tensor("fc1T", [C, HID0], F32, kind="ExternalInput").ap()
    fc2T = nc.dram_tensor("fc2T", [HID0, C], F32, kind="ExternalInput").ap()
    prows = {}
    for nm, n in [("proj_b", C), ("fc1_b", HID0), ("fc2_b", C)]:
        prows[nm] = nc.dram_tensor("r_" + nm, [1, n], F32, kind="ExternalInput").ap()
    pcols = {}
    for nm, n in [("gamma1", NCH), ("gamma2", NCH), ("ln1_g", NCH), ("ln1_b", NCH),
                  ("ln2_g", NCH), ("ln2_b", NCH), ("lnh_g", NH0), ("lnh_b", NH0)]:
        pcols[nm] = nc.dram_tensor("c_" + nm, [128, n], F32, kind="ExternalInput").ap()
    ident = nc.dram_tensor("ident", [128, 128], F32, kind="ExternalInput").ap()

    X3o = nc.dram_tensor("X3o", [C, TC], F32, kind="ExternalOutput").ap()
    max1o = nc.dram_tensor("max1o", [1, 1], F32, kind="ExternalOutput").ap()

    with ExitStack() as stack:
        p = Prog(nc, stack)
        k = build_common(p)
        projbf = p.sbuf("projbf", [128, NCH * C], BF16)
        fc1bf = p.sbuf("fc1bf", [128, NCH * HID0], BF16)
        fc2buf = p.sbuf("fc2buf", [128, 2 * C], BF16)
        fc2T_bf_hbm = nc.dram_tensor("fc2T_bf", [HID0, C], BF16, kind="Internal").ap()
        x0s = p.sbuf("x0s", [128, NCH * SL], F32)
        x1s = p.sbuf("x1s", [128, NCH * SL], F32)
        Ps = p.sbuf("Ps", [128, NCH * SL], F32)
        ybf = p.sbuf("ybf", [128, NCH * SL], BF16)
        hbf = p.sbuf("hbf", [128, NH0 * SL], BF16)
        sqb = p.sbuf("sqb", [128, SL], F32)
        hsq = p.sbuf("hsq", [128, SL], F32)
        identt = p.sbuf("identt", [128, 128], F32)
        mxs = p.sbuf("mxs", [128, NCH * 4], F32)
        mxcol = p.sbuf("mxcol", [128, 1], F32)
        m11 = p.sbuf("m11", [1, 1], F32)
        browt = {}
        for nm, n in [("proj_b", C), ("fc1_b", HID0), ("fc2_b", C)]:
            browt[nm] = p.sbuf("bt_" + nm, [1, n], BF16)
        colt = {}
        for nm in pcols:
            colt[nm] = p.sbuf("ct_" + nm, [128, pcols[nm].shape[1]], F32)
        pb = [p.psum("pb%d" % i, [128, SL], F32) for i in range(8)]
        ps_pool = {"stat1": pb[6][0:1, :], "stat2": pb[7][0:1, :],
                   "outerR": pb[4][:], "outerB": pb[5][:]}

        ld = []
        for c in range(NCH):
            ld.append(cast_load(p, projbf[:, c * C:(c + 1) * C],
                                projT[c * 128:(c + 1) * 128, :],
                                x0s[:, 0:C], deps=ld[-1:]))
            ld.append(cast_load(p, fc1bf[:, c * HID0:(c + 1) * HID0],
                                fc1T[c * 128:(c + 1) * 128, :],
                                Ps[:, 0:HID0],
                                deps=ld[-1:]))
        fc2cast = None
        for c in range(NH0):
            tc1 = p.dma("sy", x1s[:, 0:C], fc2T[c * 128:(c + 1) * 128, :],
                        deps=[fc2cast] if fc2cast else [])
            tc2 = p.emit("v", lambda e: e.tensor_copy(ybf[:, 0:C], x1s[:, 0:C]),
                         deps=[tc1])
            fc2cast = p.dma("sy", fc2T_bf_hbm[c * 128:(c + 1) * 128, :],
                            ybf[:, 0:C], deps=[tc2])
        for nm in browt:
            n = browt[nm].shape[1]
            ld.append(cast_load(p, browt[nm][:], prows[nm],
                                x0s[0:1, 0:n], deps=ld[-1:]))
        for nm in colt:
            ld.append(p.dma("sy", colt[nm][:], pcols[nm]))
        ld.append(p.dma("sy", identt[:], ident))
        ld += k["ones_tok"]
        ld.append(fc2cast)
        preload = list(ld)

        NS = TC // SL
        slice_end = list(preload)
        for s in range(NS):
            t0 = s * SL
            x0_toks = []
            for c in range(NCH):
                x0_toks.append(p.dma(
                    "sy", x0s[:, c * SL:(c + 1) * SL],
                    xT[c * 128:(c + 1) * 128, t0:t0 + SL],
                    deps=slice_end))
            y_toks = emit_ln(p, k, lambda c: x0s[:, c * SL:(c + 1) * SL], NCH,
                             ps_pool, x0_toks, sqb, None,
                             g_col=colt["ln1_g"], b_col=colt["ln1_b"],
                             dst=lambda c: ybf[:, c * SL:(c + 1) * SL])
            x1_toks = []
            P_toks = []
            for oc in range(NCH):
                bank = pb[oc % 2]
                def mm_proj(e, oc=oc, bank=bank):
                    for kc in range(NCH):
                        w = projbf[:, kc * C + oc * 128: kc * C + (oc + 1) * 128]
                        e.ldweights(w)
                        e.matmul(bank[:], w, ybf[:, kc * SL:(kc + 1) * SL],
                                 start=(kc == 0), stop=False)
                    e.ldweights(browt["proj_b"][0:1, oc * 128:(oc + 1) * 128])
                    return e.matmul(bank[:], browt["proj_b"][0:1, oc * 128:(oc + 1) * 128],
                                    k["ones_row_b"][:], start=False, stop=True)
                dep_prev = [x1_toks[oc - 2]] if oc >= 2 else []
                tm = p.emit("t", mm_proj, deps=y_toks + dep_prev)
                tx1 = p.emit("v", lambda e, oc=oc, bank=bank: e.scalar_tensor_tensor(
                    x1s[:, oc * SL:(oc + 1) * SL], bank[:],
                    colt["gamma1"][:, oc:oc + 1], x0s[:, oc * SL:(oc + 1) * SL],
                    OP.mult, OP.add), deps=[tm])
                x1_toks.append(tx1)
                P_toks.append(p.emit("v", lambda e, oc=oc: e.tensor_add(
                    Ps[:, oc * SL:(oc + 1) * SL], x0s[:, oc * SL:(oc + 1) * SL],
                    x1s[:, oc * SL:(oc + 1) * SL]), deps=[tx1]))
            z_toks = emit_ln(p, k, lambda c: x1s[:, c * SL:(c + 1) * SL], NCH,
                             ps_pool, x1_toks, sqb, None,
                             g_col=colt["ln2_g"], b_col=colt["ln2_b"],
                             dst=lambda c: ybf[:, c * SL:(c + 1) * SL])
            h_toks = []
            stat_toks = []
            for oc in range(NH0):
                bank = pb[oc % 2]
                def mm_fc1(e, oc=oc, bank=bank):
                    for kc in range(NCH):
                        w = fc1bf[:, kc * HID0 + oc * 128: kc * HID0 + (oc + 1) * 128]
                        e.ldweights(w)
                        e.matmul(bank[:], w, ybf[:, kc * SL:(kc + 1) * SL],
                                 start=(kc == 0), stop=False)
                    e.ldweights(browt["fc1_b"][0:1, oc * 128:(oc + 1) * 128])
                    return e.matmul(bank[:], browt["fc1_b"][0:1, oc * 128:(oc + 1) * 128],
                                    k["ones_row_b"][:], start=False, stop=True)
                dep_prev = [h_toks[oc - 2]] if oc >= 2 else []
                tm = p.emit("t", mm_fc1, deps=z_toks + dep_prev)
                th = p.emit("s", lambda e, oc=oc, bank=bank: e.activation(
                    hbf[:, oc * SL:(oc + 1) * SL], bank[:], AF.Gelu), deps=[tm])
                h_toks.append(th)
                tq = p.emit("s", lambda e, oc=oc: e.activation(
                    hsq[:], hbf[:, oc * SL:(oc + 1) * SL], AF.Square),
                    deps=[th] + (stat_toks[-1:] if stat_toks else []))
                ta = p.emit("t", lambda e, oc=oc: e.matmul(
                    ps_pool["stat1"], k["ones_col_b"][:],
                    hbf[:, oc * SL:(oc + 1) * SL],
                    start=(oc == 0), stop=(oc == NH0 - 1)), deps=[th])
                tb = p.emit("t", lambda e, oc=oc: e.matmul(
                    ps_pool["stat2"], k["ones_col_f"][:], hsq[:],
                    start=(oc == 0), stop=(oc == NH0 - 1)), deps=[tq])
                stat_toks = [ta, tb]
            hn_toks = emit_ln_rows_apply_only(
                p, k, HID0, ps_pool, stat_toks,
                src=lambda c: hbf[:, c * SL:(c + 1) * SL],
                dst=lambda c: hbf[:, c * SL:(c + 1) * SL],
                g_col=colt["lnh_g"], b_col=colt["lnh_b"], nch=NH0,
                extra_apply_deps=h_toks)
            fc2_toks = emit_fc2(p, k, ps_pool, pb, fc2T_bf_hbm, fc2buf, hbf,
                                browt["fc2_b"], hn_toks, fc2cast)
            Pf_toks = []
            for oc in range(NCH):
                Pf_toks.append(p.emit("v", lambda e, oc=oc: e.scalar_tensor_tensor(
                    Ps[:, oc * SL:(oc + 1) * SL], pb[oc][:],
                    colt["gamma2"][:, oc:oc + 1], Ps[:, oc * SL:(oc + 1) * SL],
                    OP.mult, OP.add), deps=[fc2_toks[oc], P_toks[oc]]))
            st_toks = []
            def post_out(c, tok):
                td = p.dma("sy", X3o[c * 128:(c + 1) * 128, t0:t0 + SL],
                           x1s[:, c * SL:(c + 1) * SL], deps=[tok])
                tmx = p.emit("v", lambda e, c=c, s=s: e.reduce_max(
                    mxs[:, s * NCH + c: s * NCH + c + 1],
                    x1s[:, c * SL:(c + 1) * SL], AX,
                    apply_absolute_value=True), deps=[tok], fence=True)
                st_toks.append(td)
                return tmx
            emit_ln(p, k, lambda c: Ps[:, c * SL:(c + 1) * SL], NCH,
                    ps_pool, Pf_toks, sqb, post_out,
                    dst=lambda c: x1s[:, c * SL:(c + 1) * SL])
            slice_end = [p.token(e) for e in ENGINES]
        tfm = p.emit("v", lambda e: e.reduce_max(mxcol[:], mxs[:], AX),
                     deps=slice_end, fence=True)
        ttr = p.emit("t", lambda e: e.transpose(pb[0][0:1, 0:128], mxcol[:],
                                                identt[:]), deps=[tfm])
        tfin = p.emit("v", lambda e: e.reduce_max(m11[:], pb[0][0:1, 0:128], AX),
                      deps=[ttr], fence=True)
        tdm = p.dma("sy", max1o, m11[:], deps=[tfin])
        p.gap("gp", [p.token("sy")])
        p.finalize()
    return nc


def emit_ln_rows_apply_only(p, k, nc_total, ps_pool, stat_toks, src, dst,
                            g_col, b_col, nch, extra_apply_deps=()):
    rows = k["rows"]
    ncf = float(nc_total)
    rM = rows[0:1, 0:SL]
    rQ = rows[0:1, SL:2 * SL]
    rRS = rows[0:1, 2 * SL:3 * SL]
    rT = rows[0:1, 3 * SL:4 * SL]
    rMRS = rows[0:1, 4 * SL:5 * SL]
    st1, st2 = ps_pool["stat1"], ps_pool["stat2"]
    t1 = p.emit("v", lambda e: e.tensor_scalar(rM, st1, -1.0 / ncf, None, OP.mult),
                deps=stat_toks)
    t2 = p.emit("v", lambda e: e.tensor_copy(rQ, st2), deps=stat_toks)
    t3 = p.emit("v", lambda e: e.tensor_mul(rT, rM, rM), deps=[t1])
    t4 = p.emit("v", lambda e: e.scalar_tensor_tensor(
        rQ, rQ, 1.0 / ncf, rT, OP.mult, OP.subtract), deps=[t2, t3])
    t4b = p.emit("v", lambda e: e.tensor_scalar(rQ, rQ, EPS, None, OP.add),
                 deps=[t4])
    t5 = p.emit("s", lambda e: e.activation(rT, rQ, AF.Sqrt), deps=[t4b])
    t6 = p.emit("v", lambda e: e.reciprocal(rRS, rT), deps=[t5], fence=True)
    t7 = p.emit("v", lambda e: e.tensor_mul(rMRS, rM, rRS), deps=[t6, t1])
    psR, psB = ps_pool["outerR"], ps_pool["outerB"]
    tR = p.emit("t", lambda e: e.matmul(psR, k["ones_row_f"][0:1, 0:128], rRS,
                                        start=True, stop=True), deps=[t6])
    tB = p.emit("t", lambda e: e.matmul(psB, k["ones_row_f"][0:1, 0:128], rMRS,
                                        start=True, stop=True), deps=[t7])
    out_toks = []
    tmp = k["tmp"]
    for c in range(nch):
        dep_ex = [extra_apply_deps[c]] if extra_apply_deps else []
        tv1 = p.emit("v", lambda e, c=c: e.tensor_mul(tmp[:], src(c), psR),
                     deps=[tR, tB] + dep_ex)
        tv2 = p.emit("v", lambda e, c=c: e.tensor_add(tmp[:], tmp[:], psB),
                     deps=[tv1])
        tok = p.emit("v", lambda e, c=c: e.tensor_scalar(
            dst(c), tmp[:], g_col[:, c:c + 1], b_col[:, c:c + 1],
            OP.mult, OP.add), deps=[tv2])
        out_toks.append(tok)
    return out_toks


def emit_fc2(p, k, ps_pool, pb, fc2T_bf_hbm, fc2buf, hbf, fc2b_row, hn_toks,
             fc2cast):
    mm_toks = [None, None]
    last = None
    for kc in range(NH0):
        par = kc % 2
        dep = [fc2cast] + ([mm_toks[par]] if mm_toks[par] else [])
        tl = p.dma("sy", fc2buf[:, par * C:(par + 1) * C],
                   fc2T_bf_hbm[kc * 128:(kc + 1) * 128, :], deps=dep)
        def mm(e, kc=kc, par=par):
            r = None
            for oc in range(NCH):
                w = fc2buf[:, par * C + oc * 128: par * C + (oc + 1) * 128]
                e.ldweights(w)
                r = e.matmul(pb[oc][:], w, hbf[:, kc * SL:(kc + 1) * SL],
                             start=(kc == 0), stop=False)
            if kc == NH0 - 1:
                for oc in range(NCH):
                    e.ldweights(fc2b_row[0:1, oc * 128:(oc + 1) * 128])
                    r = e.matmul(pb[oc][:], fc2b_row[0:1, oc * 128:(oc + 1) * 128],
                                 k["ones_row_b"][:], start=False, stop=True)
            return r
        extra = [hn_toks[-1]] if kc == 0 else []
        tm = p.emit("t", mm, deps=[tl, hn_toks[kc]] + extra)
        mm_toks[par] = tm
        last = tm
    return [last] * NCH



def emit_weight_quant(p, k, nat_dram, O, I, qT, identt, pbA, pbB, name,
                      deps=()):
    nco, nci = O // 128, I // 128
    natb = p.sbuf(name + "_nat", [128, I], F32)
    scl = p.sbuf(name + "_scl", [128, nco], F32)
    rsg = p.sbuf(name + "_rsg", [128, nco], F32)
    qi8 = p.sbuf(name + "_qi8", [128, I], I8)
    ld_prev = None
    out_toks = []
    bank_free = [None, None]
    for oc in range(nco):
        dep = list(deps) + ([ld_prev] if ld_prev else [])
        tl = p.dma("sy", natb[:],
                   nat_dram[oc * 128:(oc + 1) * 128, :], deps=dep)
        trm = p.emit("v", lambda e, oc=oc: e.reduce_max(
            scl[:, oc:oc + 1], natb[:], AX,
            apply_absolute_value=True), deps=[tl], fence=True)
        ts1 = p.emit("v", lambda e, oc=oc: e.tensor_scalar(
            scl[:, oc:oc + 1], scl[:, oc:oc + 1], 1.0 / 127.0, None, OP.mult),
            deps=[trm])
        ts2 = p.emit("v", lambda e, oc=oc: e.reciprocal(
            rsg[:, oc:oc + 1], scl[:, oc:oc + 1]), deps=[ts1], fence=True)
        tq = p.emit("v", lambda e, oc=oc: e.tensor_scalar(
            qi8[:], natb[:], rsg[:, oc:oc + 1], None,
            OP.mult), deps=[ts2])
        tb = p.emit("v", lambda e: e.tensor_copy(natb[:], qi8[:]), deps=[tq],
                    fence=True)
        ld_prev = tq
        banks = [pbA, pbB]
        for ic in range(nci):
            bi = ic % 2
            bk = banks[bi]
            dep2 = [tb] + ([bank_free[bi]] if bank_free[bi] else [])
            tt = p.emit("t", lambda e, ic=ic, bk=bk: e.transpose(
                bk[:, 0:128], natb[:, ic * 128:(ic + 1) * 128], identt[:]),
                deps=dep2)
            tv = p.emit("v", lambda e, ic=ic, oc=oc, bk=bk: e.tensor_copy(
                qT[:, ic * O + oc * 128: ic * O + (oc + 1) * 128],
                bk[:, 0:128]), deps=[tt])
            bank_free[bi] = tv
            out_toks.append(tv)
        ld_prev = out_toks[-1]
    return scl, out_toks[-2:]


def emit_global_scale(p, k, maxrow_t, clip_t, dep_toks, sx_col, rsx_col, name):
    rows = k["rows"]
    g = rows[0:1, 5 * SL:5 * SL + 1]
    sg = rows[0:1, 5 * SL + 1:5 * SL + 2]
    sx = rows[0:1, 5 * SL + 2:5 * SL + 3]
    rsx = rows[0:1, 5 * SL + 3:5 * SL + 4]
    t1 = p.emit("v", lambda e: e.reduce_max(g, maxrow_t[:], AX), deps=dep_toks, fence=True)
    t2 = p.emit("s", lambda e: e.activation(sg, clip_t[:], AF.Sigmoid),
                deps=dep_toks)
    t3 = p.emit("v", lambda e: e.tensor_mul(sx, g, sg), deps=[t1, t2])
    t4 = p.emit("v", lambda e: e.tensor_scalar(sx, sx, 1.0 / 127.0, None, OP.mult),
                deps=[t3])
    t5 = p.emit("v", lambda e: e.reciprocal(rsx, sx), deps=[t4], fence=True)
    psR = p._ps_repl
    tm1 = p.emit("t", lambda e: e.matmul(psR[:, 0:1], k["ones_row_f"][0:1, 0:128],
                                         sx, start=True, stop=True), deps=[t4])
    tc1 = p.emit("v", lambda e: e.tensor_copy(sx_col[:], psR[:, 0:1]), deps=[tm1])
    tm2 = p.emit("t", lambda e: e.matmul(psR[:, 1:2], k["ones_row_f"][0:1, 0:128],
                                         rsx, start=True, stop=True),
                 deps=[t5, tc1])
    tc2 = p.emit("v", lambda e: e.tensor_copy(rsx_col[:], psR[:, 1:2]), deps=[tm2])
    return [tc1, tc2]



DEBUG_B = False
DEBUG_B_QX = False
DEBUG_B_SGUP = False

def build_launch_B():
    nc = bass.Bass("TRN2", target_bir_lowering=False, detect_race_conditions=False)
    X3i = nc.dram_tensor("X3i", [C, TC], F32, kind="ExternalInput").ap()
    max1r = nc.dram_tensor("max1r", [1, NCORE], F32, kind="ExternalInput").ap()
    clipx = nc.dram_tensor("clipx", [1, 1], F32, kind="ExternalInput").ap()
    wg_nat = nc.dram_tensor("wg_nat", [IFF, C], F32, kind="ExternalInput").ap()
    wu_nat = nc.dram_tensor("wu_nat", [IFF, C], F32, kind="ExternalInput").ap()
    ident = nc.dram_tensor("ident", [128, 128], F32, kind="ExternalInput").ap()
    hho = nc.dram_tensor("hho", [IFF, TC], F32, kind="ExternalOutput").ap()
    max2o = nc.dram_tensor("max2o", [1, 1], F32, kind="ExternalOutput").ap()
    if DEBUG_B or DEBUG_B_QX or DEBUG_B_SGUP:
        dbg_qx = nc.dram_tensor("dbg_qx", [C, TC], F32, kind="ExternalOutput").ap()
        dbg_sg = nc.dram_tensor("dbg_sg", [IFF, TC], F32, kind="ExternalOutput").ap()
        dbg_up = nc.dram_tensor("dbg_up", [IFF, TC], F32, kind="ExternalOutput").ap()

    with ExitStack() as stack:
        p = Prog(nc, stack)
        k = build_common(p)
        pb = [p.psum("pb%d" % i, [128, SL], F32) for i in range(8)]
        p._ps_repl = pb[5]
        qTg = p.sbuf("qTg", [128, NCH * IFF], BF16)
        qTu = p.sbuf("qTu", [128, NCH * IFF], BF16)

        X3s = p.sbuf("X3s", [128, NCH * SL], F32)
        qxi8 = p.sbuf("qxi8", [128, NCH * SL], I8)
        qxbf = p.sbuf("qxbf", [128, NCH * SL], BF16)
        sgf = p.sbuf("sgf", [128, SL], F32)
        upf = p.sbuf("upf", [128, SL], F32)
        hhs = p.sbuf("hhs", [128, SL], F32)
        mxacc = p.sbuf("mxacc", [128, NIF], F32)
        mxtmp = p.sbuf("mxtmp", [128, 1], F32)
        mxcol = p.sbuf("mxcol", [128, 1], F32)
        m11 = p.sbuf("m11", [1, 1], F32)
        max1t = p.sbuf("max1t", [1, NCORE], F32)
        clipt = p.sbuf("clipt", [1, 1], F32)
        sx_col = p.sbuf("sx_col", [128, 1], F32)
        rsx_col = p.sbuf("rsx_col", [128, 1], F32)
        identt = p.sbuf("identt", [128, 128], F32)
        escg = p.sbuf("escg", [128, NIF], F32)
        escu = p.sbuf("escu", [128, NIF], F32)

        ld = [p.dma("sy", max1t[:], max1r), p.dma("sy", clipt[:], clipx),
              p.dma("sy", identt[:], ident),
              p.emit("v", lambda e: e.memset(mxacc[:], 0.0))] + k["ones_tok"]
        sc_toks = emit_global_scale(p, k, max1t, clipt, ld, sx_col, rsx_col, "x")
        sclg, tg_toks = emit_weight_quant(p, k, wg_nat, IFF, C, qTg,
                                          identt, pb[2], pb[3], "g")
        sclu, tu_toks = emit_weight_quant(p, k, wu_nat, IFF, C, qTu,
                                          identt, pb[2], pb[3], "u",
                                          deps=tg_toks)
        te1 = p.emit("v", lambda e: e.tensor_scalar(
            escg[:], sclg[:], sx_col[:, 0:1], None, OP.mult),
            deps=sc_toks + tg_toks)
        te2 = p.emit("v", lambda e: e.tensor_scalar(
            escu[:], sclu[:], sx_col[:, 0:1], None, OP.mult),
            deps=sc_toks + tu_toks)
        prep = tg_toks + tu_toks + sc_toks + [te1, te2]

        NS = TC // SL
        slice_end = list(prep)

        def BAR():
            return [p.token(e) for e in ENGINES]

        for s in range(NS):
            t0 = s * SL
            x_toks = []
            for c in range(NCH):
                tl = p.dma("sy", X3s[:, c * SL:(c + 1) * SL],
                           X3i[c * 128:(c + 1) * 128, t0:t0 + SL],
                           deps=slice_end)
                tq = p.emit("v", lambda e, c=c: e.tensor_scalar(
                    qxi8[:, c * SL:(c + 1) * SL], X3s[:, c * SL:(c + 1) * SL],
                    rsx_col[:, 0:1], None, OP.mult), deps=[tl])
                tb = p.emit("v", lambda e, c=c: e.tensor_copy(
                    qxbf[:, c * SL:(c + 1) * SL],
                    qxi8[:, c * SL:(c + 1) * SL]), deps=[tq])
                x_toks.append(tb)
            b0 = BAR()
            hh_toks = []
            gbar = b0
            for og in range(0, NIF, 4):
                ocs = list(range(og, min(og + 4, NIF)))
                mm_toks = []
                for j, oc in enumerate(ocs):
                    bk_g = pb[j]
                    bk_u = pb[4 + j]
                    def mm_g(e, oc=oc, bk=bk_g):
                        r = None
                        for kc in range(NCH):
                            w = qTg[:, kc * IFF + oc * 128: kc * IFF + (oc + 1) * 128]
                            e.ldweights(w)
                            r = e.matmul(bk[:], w, qxbf[:, kc * SL:(kc + 1) * SL],
                                         start=(kc == 0), stop=(kc == NCH - 1))
                        return r
                    def mm_u(e, oc=oc, bk=bk_u):
                        r = None
                        for kc in range(NCH):
                            w = qTu[:, kc * IFF + oc * 128: kc * IFF + (oc + 1) * 128]
                            e.ldweights(w)
                            r = e.matmul(bk[:], w, qxbf[:, kc * SL:(kc + 1) * SL],
                                         start=(kc == 0), stop=(kc == NCH - 1))
                        return r
                    mm_toks.append(p.emit("t", mm_g, deps=gbar))
                    mm_toks.append(p.emit("t", mm_u, deps=gbar))
                b1 = BAR()
                for j, oc in enumerate(ocs):
                    tsg = p.emit("s", lambda e, oc=oc, bk=pb[j]: e.activation(
                        sgf[:], bk[:], AF.Silu, scale=escg[:, oc:oc + 1]),
                        deps=b1)
                    tup = p.emit("s", lambda e, oc=oc, bk=pb[4 + j]: e.activation(
                        upf[:], bk[:], AF.Copy, scale=escu[:, oc:oc + 1]),
                        deps=b1 + [tsg])
                    thh = p.emit("v", lambda e: e.tensor_mul(hhs[:], sgf[:], upf[:]),
                                 deps=[tsg, tup])
                    td = p.dma("sy", hho[oc * 128:(oc + 1) * 128, t0:t0 + SL],
                               hhs[:], deps=[thh])
                    tr = p.emit("v", lambda e: e.reduce_max(
                        mxtmp[:], hhs[:], AX, apply_absolute_value=True),
                        deps=[thh], fence=True)
                    tm = p.emit("v", lambda e, oc=oc: e.tensor_max(
                        mxacc[:, oc:oc + 1], mxacc[:, oc:oc + 1], mxtmp[:]),
                        deps=[tr])
                    b1 = BAR()
                    hh_toks += [tm, td]
                gbar = BAR()
            slice_end = BAR()
        tfm = p.emit("v", lambda e: e.reduce_max(mxcol[:], mxacc[:], AX),
                     deps=slice_end, fence=True)
        ttr = p.emit("t", lambda e: e.transpose(pb[0][0:1, 0:128], mxcol[:],
                                                identt[:]), deps=[tfm])
        tfin = p.emit("v", lambda e: e.reduce_max(m11[:], pb[0][0:1, 0:128], AX),
                      deps=[ttr], fence=True)
        p.dma("sy", max2o, m11[:], deps=[tfin])
        p.gap("gp", [p.token("sy")])
        p.finalize()
    return nc


def qi8_v(t):
    return t[:]



def build_launch_C():
    SLC = 256
    nc = bass.Bass("TRN2", target_bir_lowering=False, detect_race_conditions=False)
    X3i = nc.dram_tensor("X3i", [C, TC], F32, kind="ExternalInput").ap()
    X3b = nc.dram_tensor("X3b", [128, NCH], F32, kind="ExternalInput").ap()
    hhi = nc.dram_tensor("hhi", [IFF, TC], F32, kind="ExternalInput").ap()
    hhb = nc.dram_tensor("hhb", [128, NIF], F32, kind="ExternalInput").ap()
    max2r = nc.dram_tensor("max2r", [1, NCORE], F32, kind="ExternalInput").ap()
    clipd = nc.dram_tensor("clipd", [1, 1], F32, kind="ExternalInput").ap()
    wd_nat = nc.dram_tensor("wd_nat", [C, IFF], F32, kind="ExternalInput").ap()
    Wk = nc.dram_tensor("Wk", [C, C], F32, kind="ExternalInput").ap()
    Wv = nc.dram_tensor("Wv", [C, C], F32, kind="ExternalInput").ap()
    Wr = nc.dram_tensor("Wr", [C, C], F32, kind="ExternalInput").ap()
    pcols = {}
    for nm in ["tmk", "tmv", "tmr", "tdec"]:
        pcols[nm] = nc.dram_tensor("c_" + nm, [128, NCH], F32,
                                   kind="ExternalInput").ap()
    bmask = nc.dram_tensor("bmask", [128, 1], F32, kind="ExternalInput").ap()
    ident = nc.dram_tensor("ident", [128, 128], F32, kind="ExternalInput").ap()
    X4o = nc.dram_tensor("X4o", [C, TC], F32, kind="ExternalOutput").ap()
    u_o = nc.dram_tensor("u_o", [C, TC], BF16, kind="ExternalOutput").ap()
    e_o = nc.dram_tensor("e_o", [C, TC], BF16, kind="ExternalOutput").ap()
    r_o = nc.dram_tensor("r_o", [C, TC], BF16, kind="ExternalOutput").ap()
    summ_o = nc.dram_tensor("summ_o", [128, 2 * NCH], F32, kind="ExternalOutput").ap()

    with ExitStack() as stack:
        p = Prog(nc, stack)
        k = build_common(p)
        pb = [p.psum("pb%d" % i, [128, SL], F32) for i in range(8)]
        p._ps_repl = pb[5]
        ps_pool = {"stat1": pb[6], "stat2": pb[7], "outerR": pb[4], "outerB": pb[5]}
        qTd = p.sbuf("qTd", [128, NIF * C], BF16)
        Wkbf = p.sbuf("Wkbf", [128, NCH * C], BF16)
        Wvbf = p.sbuf("Wvbf", [128, NCH * C], BF16)
        Wrbf = p.sbuf("Wrbf", [128, NCH * C], BF16)
        X3s = p.sbuf("X3s", [128, NCH * SLC], F32)
        P2 = p.sbuf("P2", [128, NCH * SLC], F32)
        hstg = p.sbuf("hstg", [128, 2 * SLC], F32)
        qhi8 = p.sbuf("qhi8", [128, SLC], I8)
        qhbf = p.sbuf("qhbf", [128, NIF * SLC], BF16)
        X4e = p.sbuf("X4e", [128, NCH * (SLC + 1)], F32)
        dbuf = p.sbuf("dbuf", [128, SLC], F32)
        mixbf = p.sbuf("mixbf", [128, 3 * NCH * SLC], BF16)
        ebuf = p.sbuf("ebuf", [128, SLC], F32)
        ubf_t = p.sbuf("ubf_t", [128, SLC], BF16)
        ebf_t = p.sbuf("ebf_t", [128, SLC], BF16)
        rbf_t = p.sbuf("rbf_t", [128, SLC], BF16)
        X4bcol = p.sbuf("X4bcol", [128, NCH], F32)
        bmt = p.sbuf("bmt", [128, 1], F32)
        max2t = p.sbuf("max2t", [1, NCORE], F32)
        clipt = p.sbuf("clipt", [1, 1], F32)
        sh_col = p.sbuf("sh_col", [128, 1], F32)
        rsh_col = p.sbuf("rsh_col", [128, 1], F32)
        escd = p.sbuf("escd", [128, NCH], F32)
        colt = {nm: p.sbuf("ct_" + nm, [128, NCH], F32) for nm in pcols}
        Wcol = p.sbuf("Wcol", [128, NCH], F32)
        Wb = p.sbuf("Wb", [128, TC], F32)
        uc = p.sbuf("uc", [128, TC], BF16)
        ec = p.sbuf("ec", [128, TC], BF16)
        asc = p.sbuf("asc", [128, TC], F32)
        summ = p.sbuf("summ", [128, 2 * NCH], F32)
        sqc = p.sbuf("sqc", [128, SLC], F32)
        qhb8 = p.sbuf("qhb8", [128, NIF], I8)
        qhbcol = p.sbuf("qhbcol", [128, NIF], BF16)
        X3bcol = p.sbuf("X3bcol", [128, NCH], F32)
        hhbcol = p.sbuf("hhbcol", [128, NIF], F32)

        identtC = p.sbuf("identtC", [128, 128], F32)
        ld = [p.dma("sy", max2t[:], max2r), p.dma("sy", clipt[:], clipd),
              p.dma("sy", bmt[:], bmask),
              p.dma("sy", identtC[:], ident),
              p.emit("v", lambda e: e.memset(Wb[:], 0.0))] + k["ones_tok"]

        for nm in pcols:
            ld.append(p.dma("sy", colt[nm][:], pcols[nm]))
        for c in range(NCH):
            ld.append(cast_load(p, Wkbf[:, c * C:(c + 1) * C],
                                Wk[c * 128:(c + 1) * 128, :], X3s[:, 0:C],
                                deps=ld[-1:]))
            ld.append(cast_load(p, Wvbf[:, c * C:(c + 1) * C],
                                Wv[c * 128:(c + 1) * 128, :], X3s[:, 0:C],
                                deps=ld[-1:]))
            ld.append(cast_load(p, Wrbf[:, c * C:(c + 1) * C],
                                Wr[c * 128:(c + 1) * 128, :], X3s[:, 0:C],
                                deps=ld[-1:]))
        tb1 = p.dma("sy", X3bcol[:], X3b)
        tb2 = p.dma("sy", hhbcol[:], hhb)
        ld += [tb1, tb2]
        sc_toks = emit_global_scale(p, k, max2t, clipt, ld, sh_col, rsh_col, "h")
        scld, td_toks = emit_weight_quant(p, k, wd_nat, C, IFF, qTd,
                                          identtC, pb[2], pb[3], "d")
        te1 = p.emit("v", lambda e: e.tensor_scalar(
            escd[:], scld[:], sh_col[:, 0:1], None, OP.mult),
            deps=sc_toks + td_toks)
        tw1 = p.emit("s", lambda e: e.activation(Wcol[:], colt["tdec"][:], AF.Exp),
                     deps=ld)
        tw2 = p.emit("v", lambda e: e.tensor_scalar(Wcol[:], Wcol[:], -1.0, None,
                                                    OP.mult), deps=[tw1])
        tw3 = p.emit("s", lambda e: e.activation(Wcol[:], Wcol[:], AF.Exp),
                     deps=[tw2])
        prep = sc_toks + td_toks + [te1, tw3]

        tqb = p.emit("v", lambda e: e.tensor_scalar(
            qhb8[:], hhbcol[:], rsh_col[:, 0:1], None, OP.mult), deps=prep)
        tqb2 = p.emit("v", lambda e: e.tensor_copy(qhbcol[:], qhb8[:]), deps=[tqb])
        pb2_toks = []
        for oc in range(NCH):
            bank = pb[oc % 2]
            def mmb(e, oc=oc, bank=bank):
                r = None
                for kc in range(NIF):
                    w = qTd[:, kc * C + oc * 128: kc * C + (oc + 1) * 128]
                    e.ldweights(w)
                    r = e.matmul(bank[:, 0:1], w, qhbcol[:, kc:kc + 1],
                                 start=(kc == 0), stop=(kc == NIF - 1))
                return r
            dep_prev = [pb2_toks[oc - 2]] if oc >= 2 else []
            tm = p.emit("t", mmb, deps=[tqb2] + dep_prev + td_toks)
            pb2_toks.append(p.emit("v", lambda e, oc=oc, bank=bank: e.scalar_tensor_tensor(
                P2[:, oc:oc + 1], bank[:, 0:1], escd[:, oc:oc + 1],
                X3bcol[:, oc:oc + 1], OP.mult, OP.add), deps=[tm, te1]))
        bl_toks = emit_ln(p, k, lambda c: P2[:, c:c + 1], NCH, ps_pool,
                          pb2_toks, sqc, None,
                          dst=lambda c: X4bcol[:, c:c + 1], w=1)
        tbm = p.emit("v", lambda e: e.tensor_scalar(
            X4bcol[:], X4bcol[:], bmt[:, 0:1], None, OP.mult), deps=bl_toks)

        NS = TC // SLC
        slice_end = [tbm] + prep
        for s in range(NS):
            t0 = s * SLC
            x_toks = []
            for c in range(NCH):
                x_toks.append(p.dma("sy", X3s[:, c * SLC:(c + 1) * SLC],
                                    X3i[c * 128:(c + 1) * 128, t0:t0 + SLC],
                                    deps=slice_end))
            qh_toks = []
            ld_par = [None, None]
            for ic in range(NIF):
                par = ic % 2
                dep = [ld_par[par]] if ld_par[par] else []
                tl = p.dma("sy", hstg[:, par * SLC:(par + 1) * SLC],
                           hhi[ic * 128:(ic + 1) * 128, t0:t0 + SLC],
                           deps=dep + list(slice_end))
                tq = p.emit("v", lambda e, ic=ic, par=par: e.tensor_scalar(
                    qhi8[:], hstg[:, par * SLC:(par + 1) * SLC], rsh_col[:, 0:1],
                    None, OP.mult), deps=[tl] + (qh_toks[-1:] if qh_toks else []))
                tb = p.emit("v", lambda e, ic=ic: e.tensor_copy(
                    qhbf[:, ic * SLC:(ic + 1) * SLC], qhi8[:]), deps=[tq])
                qh_toks.append(tb)
                ld_par[par] = tq
            d_toks = []
            for oc in range(NCH):
                bank = pb[oc % 2]
                def mmd(e, oc=oc, bank=bank):
                    r = None
                    for kc in range(NIF):
                        w = qTd[:, kc * C + oc * 128: kc * C + (oc + 1) * 128]
                        e.ldweights(w)
                        r = e.matmul(bank[:, 0:SLC], w, qhbf[:, kc * SLC:(kc + 1) * SLC],
                                     start=(kc == 0), stop=(kc == NIF - 1))
                    return r
                dep_prev = [d_toks[oc - 2]] if oc >= 2 else []
                tm = p.emit("t", mmd, deps=qh_toks + dep_prev)
                d_toks.append(p.emit("v", lambda e, oc=oc, bank=bank: e.scalar_tensor_tensor(
                    P2[:, oc * SLC:(oc + 1) * SLC], bank[:, 0:SLC], escd[:, oc:oc + 1],
                    X3s[:, oc * SLC:(oc + 1) * SLC], OP.mult, OP.add),
                    deps=[tm, x_toks[oc]]))
            x4_st = []
            def post_out(c, tok):
                x4_st.append(p.dma("sy", X4o[c * 128:(c + 1) * 128, t0:t0 + SLC],
                                   X3s[:, c * SLC:(c + 1) * SLC], deps=[tok]))
                return tok
            emit_ln(p, k, lambda c: P2[:, c * SLC:(c + 1) * SLC], NCH, ps_pool,
                    d_toks, sqc, post_out,
                    dst=lambda c: X3s[:, c * SLC:(c + 1) * SLC], w=SLC)
            slice_end = [p.token(e) for e in ENGINES]
        for s in range(NS):
            t0 = s * SLC
            xe_toks = []
            for c in range(NCH):
                base = c * (SLC + 1)
                if s == 0:
                    t1 = p.dma("sy", X4e[:, base + 1: base + 1 + SLC],
                               X4o[c * 128:(c + 1) * 128, 0:SLC], deps=slice_end)
                    t2 = p.emit("v", lambda e, c=c, base=base: e.tensor_copy(
                        X4e[:, base:base + 1], X4bcol[:, c:c + 1]),
                        deps=[t1, tbm] + list(slice_end))
                    xe_toks.append(t2)
                else:
                    xe_toks.append(p.dma(
                        "sy", X4e[:, base: base + 1 + SLC],
                        X4o[c * 128:(c + 1) * 128, t0 - 1:t0 + SLC],
                        deps=slice_end))
            mix_toks = []
            for c in range(NCH):
                base = c * (SLC + 1)
                sh_ap = X4e[:, base:base + SLC]
                cur = X4e[:, base + 1:base + 1 + SLC]
                tdf = p.emit("v", lambda e, cur=cur, sh_ap=sh_ap: e.tensor_sub(
                    dbuf[:], cur, sh_ap), deps=[xe_toks[c]] +
                    (mix_toks[-1:] if mix_toks else []))
                for j, nm in enumerate(["tmk", "tmv", "tmr"]):
                    mix_toks.append(p.emit(
                        "v", lambda e, c=c, j=j, nm=nm, sh_ap=sh_ap:
                        e.scalar_tensor_tensor(
                            mixbf[:, (j * NCH + c) * SLC:(j * NCH + c + 1) * SLC],
                            dbuf[:], colt[nm][:, c:c + 1], sh_ap,
                            OP.mult, OP.add), deps=[tdf]))
            kvr_st = []
            for oc in range(NCH):
                bank_k = pb[0 + (oc % 2)]
                bank_v = pb[2 + (oc % 2)]
                bank_r = pb[4 + (oc % 2)]
                def mmk(e, oc=oc, bank=bank_k, wt=Wkbf, j=0):
                    r = None
                    for kc in range(NCH):
                        w = wt[:, kc * C + oc * 128: kc * C + (oc + 1) * 128]
                        e.ldweights(w)
                        r = e.matmul(bank[:, 0:SLC], w,
                                     mixbf[:, (j * NCH + kc) * SLC:(j * NCH + kc + 1) * SLC],
                                     start=(kc == 0), stop=(kc == NCH - 1))
                    return r
                dep_prev = kvr_st[-6:] if kvr_st else []
                tk = p.emit("t", lambda e, oc=oc: mmk(e, oc, pb[0 + (oc % 2)], Wkbf, 0),
                            deps=mix_toks + dep_prev)
                tv = p.emit("t", lambda e, oc=oc: mmk(e, oc, pb[2 + (oc % 2)], Wvbf, 1),
                            deps=mix_toks + dep_prev)
                tr = p.emit("t", lambda e, oc=oc: mmk(e, oc, pb[4 + (oc % 2)], Wrbf, 2),
                            deps=mix_toks + dep_prev)
                te = p.emit("s", lambda e, oc=oc: e.activation(
                    ebuf[:], pb[0 + (oc % 2)][:, 0:SLC], AF.Exp), deps=[tk])
                teb = p.emit("v", lambda e: e.tensor_copy(ebf_t[:], ebuf[:]),
                             deps=[te])
                tu = p.emit("v", lambda e, oc=oc: e.tensor_mul(
                    ubf_t[:], ebuf[:], pb[2 + (oc % 2)][:, 0:SLC]), deps=[te, tv])
                trs = p.emit("s", lambda e, oc=oc: e.activation(
                    rbf_t[:], pb[4 + (oc % 2)][:, 0:SLC], AF.Sigmoid), deps=[tr])
                d1 = p.dma("sy", e_o[oc * 128:(oc + 1) * 128, t0:t0 + SLC],
                           ebf_t[:], deps=[teb])
                d2 = p.dma("sy", u_o[oc * 128:(oc + 1) * 128, t0:t0 + SLC],
                           ubf_t[:], deps=[tu])
                d3 = p.dma("sy", r_o[oc * 128:(oc + 1) * 128, t0:t0 + SLC],
                           rbf_t[:], deps=[trs])
                kvr_st += [d1, d2, d3]
            slice_end = [p.token(e) for e in ENGINES]
        sm_toks = []
        for c in range(NCH):
            tl1 = p.dma("sy", uc[:], u_o[c * 128:(c + 1) * 128, :], deps=slice_end)
            tl2 = p.dma("sy", ec[:], e_o[c * 128:(c + 1) * 128, :], deps=slice_end)
            twb = p.emit("v", lambda e, c=c: e.tensor_scalar(
                Wb[:], Wb[:], 0.0, Wcol[:, c:c + 1], OP.mult, OP.add),
                deps=slice_end + (sm_toks[-2:] if sm_toks else []))
            ts1 = p.emit("v", lambda e: e.tensor_tensor_scan(
                asc[:], Wb[:], uc[:], 0.0, OP.mult, OP.add), deps=[tl1, twb], fence=True)
            tc1 = p.emit("v", lambda e, c=c: e.tensor_copy(
                summ[:, c:c + 1], asc[:, TC - 1:TC]), deps=[ts1])
            ts2 = p.emit("v", lambda e: e.tensor_tensor_scan(
                asc[:], Wb[:], ec[:], 0.0, OP.mult, OP.add), deps=[tl2, tc1], fence=True)
            tc2 = p.emit("v", lambda e, c=c: e.tensor_copy(
                summ[:, NCH + c:NCH + c + 1], asc[:, TC - 1:TC]), deps=[ts2])
            sm_toks += [tc1, tc2]
            slice_end = [p.token("v"), p.token("sy")]
        p.dma("sy", summ_o, summ[:], deps=sm_toks)
        p.gap("gp", [p.token("sy")])
        p.finalize()
    return nc



DEBUG_D = False


def build_launch_D():
    nc = bass.Bass("TRN2", target_bir_lowering=False, detect_race_conditions=False)
    X4i = nc.dram_tensor("X4i", [C, TC], F32, kind="ExternalInput").ap()
    u_i = nc.dram_tensor("u_i", [C, TC], BF16, kind="ExternalInput").ap()
    e_i = nc.dram_tensor("e_i", [C, TC], BF16, kind="ExternalInput").ap()
    r_i = nc.dram_tensor("r_i", [C, TC], BF16, kind="ExternalInput").ap()
    sum_a = nc.dram_tensor("sum_a", [128, 4 * NCH], F32, kind="ExternalInput").ap()
    sum_b = nc.dram_tensor("sum_b", [128, 4 * NCH], F32, kind="ExternalInput").ap()
    wkv_m = nc.dram_tensor("wkv_m", [128, 4], F32, kind="ExternalInput").ap()
    wkv_mask = nc.dram_tensor("wkv_mask", [128, 4], F32, kind="ExternalInput").ap()
    Wo = nc.dram_tensor("Wo", [C, C], F32, kind="ExternalInput").ap()
    fc1q = nc.dram_tensor("fc1q", [C, HID1], I32, kind="ExternalInput").ap()
    fc1srep = nc.dram_tensor("fc1srep", [C, HID1], F32, kind="ExternalInput").ap()
    fc2q = nc.dram_tensor("fc2q", [HID1, C], I32, kind="ExternalInput").ap()
    fc2srep = nc.dram_tensor("fc2srep", [HID1, C], F32, kind="ExternalInput").ap()
    b1r = nc.dram_tensor("b1r", [1, HID1], F32, kind="ExternalInput").ap()
    b2r = nc.dram_tensor("b2r", [1, C], F32, kind="ExternalInput").ap()
    pcols = {}
    for nm in ["tdec", "tfirst", "lng", "lnb"]:
        pcols[nm] = nc.dram_tensor("c_" + nm, [128, NCH], F32,
                                   kind="ExternalInput").ap()
    wq1h = nc.dram_tensor("wq1h", [C, HID1], BF16, kind="Internal").ap()
    wq2h = nc.dram_tensor("wq2h", [HID1, C], BF16, kind="Internal").ap()
    X6o = nc.dram_tensor("X6o", [C, TC], F32, kind="ExternalOutput").ap()
    if DEBUG_D:
        dbg_rw = nc.dram_tensor("dbg_rw", [128, NCH * TC], BF16,
                                kind="ExternalOutput").ap()
        dbg_x5 = nc.dram_tensor("dbg_x5", [128, NCH * SL], F32,
                                kind="ExternalOutput").ap()
        dbg_hf = nc.dram_tensor("dbg_hf", [128, NCH * SL], BF16,
                                kind="ExternalOutput").ap()
        dbg_y = nc.dram_tensor("dbg_y", [128, NH1 * SL], BF16,
                               kind="ExternalOutput").ap()
        dbg_p4 = nc.dram_tensor("dbg_p4", [128, NCH * SL], F32,
                                kind="ExternalOutput").ap()
        dbg_p3 = nc.dram_tensor("dbg_p3", [128, NCH * SL], F32,
                                kind="ExternalOutput").ap()
        dbg_wo = nc.dram_tensor("dbg_wo", [128, NCH * C], BF16,
                                kind="ExternalOutput").ap()

    with ExitStack() as stack:
        p = Prog(nc, stack)
        k = build_common(p)
        pb = [p.psum("pb%d" % i, [128, SL], F32) for i in range(8)]
        p._ps_repl = pb[5]
        ps_pool = {"stat1": pb[6], "stat2": pb[7], "outerR": pb[4], "outerB": pb[5]}
        Wobf = p.sbuf("Wobf", [128, NCH * C], BF16)
        rwbf = p.sbuf("rwbf", [128, NCH * TC], BF16)
        X4s = p.sbuf("X4s", [128, NCH * SL], F32)
        P3 = p.sbuf("P3", [128, NCH * SL], F32)
        X5s = p.sbuf("X5s", [128, NCH * SL], F32)
        hfbf = p.sbuf("hfbf", [128, NCH * SL], BF16)
        ysil = p.sbuf("ysil", [128, NH1 * SL], BF16)
        sqb = p.sbuf("sqb", [128, SL], F32)
        colt = {nm: p.sbuf("ct_" + nm, [128, NCH], F32) for nm in pcols}
        b1bf = p.sbuf("b1bf", [1, HID1], BF16)
        b2bf = p.sbuf("b2bf", [1, C], BF16)
        Wcol = p.sbuf("WcolD", [128, NCH], F32)
        etf = p.sbuf("etf", [128, NCH], F32)
        suma_t = p.sbuf("suma_t", [128, 4 * NCH], F32)
        sumb_t = p.sbuf("sumb_t", [128, 4 * NCH], F32)
        wm_t = p.sbuf("wm_t", [128, 4], F32)
        wmask_t = p.sbuf("wmask_t", [128, 4], F32)
        Ain = p.sbuf("Ain", [128, NCH], F32)
        Bin = p.sbuf("Bin", [128, NCH], F32)
        coefc = p.sbuf("coefc", [128, NCH], F32)
        tmp8 = p.sbuf("tmp8", [128, NCH], F32)
        Wb = p.sbuf("WbD", [128, TC], F32)
        uc = ysil[:, 0:TC]
        ec = ysil[:, TC:2 * TC]
        rc = ysil[:, 2 * TC:3 * TC]
        ascx = X5s[:, 0:TC + 1]
        bscx = X4s[:, 0:TC + 1]
        numb = P3[:, 0:TC]
        denb = P3[:, TC:2 * TC]
        w1buf = p.sbuf("w1buf", [128, 2 * 8 * 128], BF16)
        w2buf = p.sbuf("w2buf", [128, 2 * 32 * 128], BF16)
        DQW = HID1 // 4
        dq32 = p.sbuf("dq32", [128, DQW], I32)
        dqqf = p.sbuf("dqqf", [128, DQW], F32)
        dqsf = p.sbuf("dqsf", [128, DQW], F32)
        dqo = p.sbuf("dqo", [128, DQW], BF16)

        ld = [p.dma("sy", suma_t[:], sum_a), p.dma("sy", sumb_t[:], sum_b),
              p.dma("sy", wm_t[:], wkv_m), p.dma("sy", wmask_t[:], wkv_mask),
              p.emit("v", lambda e: e.memset(Wb[:], 0.0))]
        ld.append(cast_load(p, b1bf[:], b1r, X4s[0:1, 0:HID1], deps=ld[-1:]))
        ld.append(cast_load(p, b2bf[:], b2r, X4s[0:1, 0:C], deps=ld[-1:]))
        for nm in pcols:
            ld.append(p.dma("sy", colt[nm][:], pcols[nm]))
        for c in range(NCH):
            ld.append(cast_load(p, Wobf[:, c * C:(c + 1) * C],
                                Wo[c * 128:(c + 1) * 128, :], X4s[:, 0:C],
                                deps=ld[-1:]))
        ld += k["ones_tok"]
        tw1 = p.emit("s", lambda e: e.activation(Wcol[:], colt["tdec"][:], AF.Exp),
                     deps=ld)
        tw2 = p.emit("v", lambda e: e.tensor_scalar(Wcol[:], Wcol[:], -1.0, None,
                                                    OP.mult), deps=[tw1])
        tw3 = p.emit("s", lambda e: e.activation(Wcol[:], Wcol[:], AF.Exp),
                     deps=[tw2])
        tet = p.emit("s", lambda e: e.activation(etf[:], colt["tfirst"][:], AF.Exp),
                     deps=ld)
        ta0 = p.emit("v", lambda e: e.memset(Ain[:], 0.0))
        tb0 = p.emit("v", lambda e: e.memset(Bin[:], 0.0))
        negexp = p.sbuf("negexp", [128, NCH], F32)
        tn1 = p.emit("s", lambda e: e.activation(negexp[:], colt["tdec"][:], AF.Exp),
                     deps=ld)
        tn2 = p.emit("v", lambda e: e.tensor_scalar(negexp[:], negexp[:], -1.0,
                                                    None, OP.mult), deps=[tn1])
        comb2 = [ta0, tb0, tn2]
        for q in range(4):
            t1 = p.emit("v", lambda e, q=q: e.tensor_scalar(
                coefc[:], negexp[:], wm_t[:, q:q + 1], None, OP.mult),
                deps=comb2)
            t2 = p.emit("s", lambda e: e.activation(coefc[:], coefc[:], AF.Exp),
                        deps=[t1])
            t3 = p.emit("v", lambda e, q=q: e.tensor_scalar(
                coefc[:], coefc[:], wmask_t[:, q:q + 1], None, OP.mult),
                deps=[t2])
            t4 = p.emit("v", lambda e, q=q: e.tensor_mul(
                tmp8[:], coefc[:], suma_t[:, q * NCH:(q + 1) * NCH]), deps=[t3])
            t5 = p.emit("v", lambda e: e.tensor_add(Ain[:], Ain[:], tmp8[:]),
                        deps=[t4])
            t6 = p.emit("v", lambda e, q=q: e.tensor_mul(
                tmp8[:], coefc[:], sumb_t[:, q * NCH:(q + 1) * NCH]), deps=[t5])
            t7 = p.emit("v", lambda e: e.tensor_add(Bin[:], Bin[:], tmp8[:]),
                        deps=[t6])
            comb2 = [t7]
        comb_done = comb2

        dq_toks = []
        prev = []
        for kc in range(NCH):
            for hf in range(HID1 // DQW):
                c0 = hf * DQW
                tl1 = p.dma("sy", dq32[:], fc1q[kc * 128:(kc + 1) * 128,
                                                c0:c0 + DQW], deps=prev)
                tl2 = p.dma("sy", dqsf[:], fc1srep[kc * 128:(kc + 1) * 128,
                                                   c0:c0 + DQW], deps=prev)
                tcv = p.emit("v", lambda e: e.tensor_copy(dqqf[:], dq32[:]),
                             deps=[tl1])
                tdq = p.emit("v", lambda e: e.scalar_tensor_tensor(
                    dqo[:], dqqf[:], -128.0, dqsf[:], OP.add, OP.mult),
                    deps=[tcv, tl2])
                td = p.dma("sy", wq1h[kc * 128:(kc + 1) * 128, c0:c0 + DQW],
                           dqo[:], deps=[tdq])
                dq_toks.append(td)
                prev = [tdq, td]
        for kc in range(NH1):
            tl1 = p.dma("sy", dq32[:, 0:C], fc2q[kc * 128:(kc + 1) * 128, :],
                        deps=prev)
            tl2 = p.dma("sy", dqsf[:, 0:C], fc2srep[kc * 128:(kc + 1) * 128, :],
                        deps=prev)
            tcv = p.emit("v", lambda e: e.tensor_copy(dqqf[:, 0:C], dq32[:, 0:C]),
                         deps=[tl1])
            tdq = p.emit("v", lambda e: e.scalar_tensor_tensor(
                dqo[:, 0:C], dqqf[:, 0:C], -128.0, dqsf[:, 0:C], OP.add, OP.mult),
                deps=[tcv, tl2])
            td = p.dma("sy", wq2h[kc * 128:(kc + 1) * 128, :], dqo[:, 0:C],
                       deps=[tdq])
            dq_toks.append(td)
            prev = [tdq, td]

        rw_toks = []
        chunk_end = comb_done + [tw3, tet]
        for c in range(NCH):
            tl1 = p.dma("sy", uc[:], u_i[c * 128:(c + 1) * 128, :], deps=chunk_end)
            tl2 = p.dma("sy", ec[:], e_i[c * 128:(c + 1) * 128, :], deps=chunk_end)
            tl3 = p.dma("sy", rc[:], r_i[c * 128:(c + 1) * 128, :], deps=chunk_end)
            twb = p.emit("v", lambda e, c=c: e.tensor_scalar(
                Wb[:], Wb[:], 0.0, Wcol[:, c:c + 1], OP.mult, OP.add),
                deps=chunk_end)
            tca = p.emit("v", lambda e, c=c: e.tensor_copy(
                ascx[:, 0:1], Ain[:, c:c + 1]), deps=chunk_end)
            tcb = p.emit("v", lambda e, c=c: e.tensor_copy(
                bscx[:, 0:1], Bin[:, c:c + 1]), deps=chunk_end)
            ts1 = p.emit("v", lambda e, c=c: e.tensor_tensor_scan(
                ascx[:, 1:TC + 1], Wb[:], uc[:], Ain[:, c:c + 1],
                OP.mult, OP.add), deps=[tl1, twb, tca], fence=True)
            ts2 = p.emit("v", lambda e, c=c: e.tensor_tensor_scan(
                bscx[:, 1:TC + 1], Wb[:], ec[:], Bin[:, c:c + 1],
                OP.mult, OP.add), deps=[tl2, twb, tcb], fence=True)
            tnum = p.emit("v", lambda e, c=c: e.scalar_tensor_tensor(
                numb[:], uc[:], etf[:, c:c + 1], ascx[:, 0:TC],
                OP.mult, OP.add), deps=[ts1])
            tden = p.emit("v", lambda e, c=c: e.scalar_tensor_tensor(
                denb[:], ec[:], etf[:, c:c + 1], bscx[:, 0:TC],
                OP.mult, OP.add), deps=[ts2])
            trc = p.emit("v", lambda e: e.reciprocal(denb[:], denb[:]),
                         deps=[tden], fence=True)
            twk = p.emit("v", lambda e: e.tensor_mul(numb[:], numb[:], denb[:]),
                         deps=[trc, tnum])
            trw = p.emit("v", lambda e, c=c: e.tensor_mul(
                rwbf[:, c * TC:(c + 1) * TC], numb[:], rc[:]), deps=[twk, tl3])
            rw_toks.append(trw)
            chunk_end = [p.token("v"), p.token("sy")]

        if DEBUG_D:
            rw_toks.append(p.dma("sy", dbg_rw, rwbf[:], deps=rw_toks))
        NS = TC // SL
        slice_end = rw_toks + dq_toks
        for s in range(NS):
            t0 = s * SL
            x4_toks = []
            for c in range(NCH):
                x4_toks.append(p.dma("sy", X4s[:, c * SL:(c + 1) * SL],
                                     X4i[c * 128:(c + 1) * 128, t0:t0 + SL],
                                     deps=slice_end))
            p3_toks = []
            for oc in range(NCH):
                bank = pb[oc % 2]
                def mmo(e, oc=oc, bank=bank, t0=t0):
                    r = None
                    for kc in range(NCH):
                        w = Wobf[:, kc * C + oc * 128: kc * C + (oc + 1) * 128]
                        e.ldweights(w)
                        r = e.matmul(bank[:], w,
                                     rwbf[:, kc * TC + t0: kc * TC + t0 + SL],
                                     start=(kc == 0), stop=(kc == NCH - 1))
                    return r
                dep_prev = [p3_toks[oc - 2]] if oc >= 2 else []
                tm = p.emit("t", mmo, deps=rw_toks + dep_prev + list(slice_end))
                p3_toks.append(p.emit("v", lambda e, oc=oc, bank=bank: e.tensor_add(
                    P3[:, oc * SL:(oc + 1) * SL], X4s[:, oc * SL:(oc + 1) * SL],
                    bank[:]), deps=[tm, x4_toks[oc]]))
            if DEBUG_D and s == 0:
                p3_toks = list(p3_toks) + [
                    p.dma("sy", dbg_p3, P3[:], deps=p3_toks),
                    p.dma("sy", dbg_wo, Wobf[:], deps=p3_toks)]
            x5_toks = emit_ln(p, k, lambda c: P3[:, c * SL:(c + 1) * SL], NCH,
                              ps_pool, p3_toks, sqb, None,
                              dst=lambda c: X5s[:, c * SL:(c + 1) * SL])
            if DEBUG_D and s == 0:
                x5_toks = list(x5_toks) + [p.dma("sy", dbg_x5, X5s[:],
                                                 deps=x5_toks)]
            hf_toks = emit_ln(p, k, lambda c: X5s[:, c * SL:(c + 1) * SL], NCH,
                              ps_pool, x5_toks, sqb, None,
                              g_col=colt["lng"], b_col=colt["lnb"],
                              dst=lambda c: hfbf[:, c * SL:(c + 1) * SL])
            if DEBUG_D and s == 0:
                hf_toks = list(hf_toks) + [p.dma("sy", dbg_hf, hfbf[:],
                                                 deps=hf_toks)]
            y_toks = []
            wl_par = [None, None]
            for oc in range(NH1):
                par = oc % 2
                dep = [wl_par[par]] if wl_par[par] else []
                tlw = [p.dma("sy",
                             w1buf[:, par * 8 * 128 + kc * 128:
                                   par * 8 * 128 + (kc + 1) * 128],
                             wq1h[kc * 128:(kc + 1) * 128,
                                  oc * 128:(oc + 1) * 128],
                             deps=dep + dq_toks) for kc in range(NCH)]
                bank = pb[2 + (oc % 2)]
                def mm1(e, oc=oc, par=par, bank=bank):
                    for kc in range(NCH):
                        w = w1buf[:, par * 8 * 128 + kc * 128: par * 8 * 128 + (kc + 1) * 128]
                        e.ldweights(w)
                        e.matmul(bank[:], w, hfbf[:, kc * SL:(kc + 1) * SL],
                                 start=(kc == 0), stop=False)
                    e.ldweights(b1bf[0:1, oc * 128:(oc + 1) * 128])
                    return e.matmul(bank[:], b1bf[0:1, oc * 128:(oc + 1) * 128],
                                    k["ones_row_b"][:], start=False, stop=True)
                dep_prev = [y_toks[oc - 2]] if oc >= 2 else []
                tm = p.emit("t", mm1, deps=hf_toks + tlw + dep_prev)
                wl_par[par] = tm
                ty = p.emit("s", lambda e, oc=oc, bank=bank: e.activation(
                    ysil[:, oc * SL:(oc + 1) * SL], bank[:], AF.Silu), deps=[tm])
                y_toks.append(ty)
            if DEBUG_D and s == 0:
                y_toks = list(y_toks) + [p.dma("sy", dbg_y, ysil[:],
                                               deps=y_toks)]
            x6_st = []
            p4_toks = []
            w2_par = [None, None]
            for oc in range(NCH):
                par = oc % 2
                dep = [w2_par[par]] if w2_par[par] else []
                tlw = [p.dma("sy",
                             w2buf[:, par * 32 * 128 + kc * 128:
                                   par * 32 * 128 + (kc + 1) * 128],
                             wq2h[kc * 128:(kc + 1) * 128,
                                  oc * 128:(oc + 1) * 128],
                             deps=dep + dq_toks) for kc in range(NH1)]
                bank = pb[oc % 2]
                def mm2(e, oc=oc, par=par, bank=bank):
                    for kc in range(NH1):
                        w = w2buf[:, par * 32 * 128 + kc * 128: par * 32 * 128 + (kc + 1) * 128]
                        e.ldweights(w)
                        e.matmul(bank[:], w, ysil[:, kc * SL:(kc + 1) * SL],
                                 start=(kc == 0), stop=False)
                    e.ldweights(b2bf[0:1, oc * 128:(oc + 1) * 128])
                    return e.matmul(bank[:], b2bf[0:1, oc * 128:(oc + 1) * 128],
                                    k["ones_row_b"][:], start=False, stop=True)
                dep_prev = [p4_toks[oc - 2]] if oc >= 2 else []
                tm = p.emit("t", mm2, deps=y_toks + tlw + dep_prev)
                w2_par[par] = tm
                p4_toks.append(p.emit("v", lambda e, oc=oc, bank=bank:
                    e.scalar_tensor_tensor(
                        P3[:, oc * SL:(oc + 1) * SL], bank[:], 0.5,
                        X5s[:, oc * SL:(oc + 1) * SL], OP.mult, OP.add),
                    deps=[tm]))
            if DEBUG_D and s == 0:
                p4_toks = list(p4_toks) + [p.dma("sy", dbg_p4, P3[:],
                                                 deps=p4_toks)]
            def out6(c, tok):
                x6_st.append(p.dma("sy", X6o[c * 128:(c + 1) * 128, t0:t0 + SL],
                                   X4s[:, c * SL:(c + 1) * SL], deps=[tok]))
                return tok
            emit_ln(p, k, lambda c: P3[:, c * SL:(c + 1) * SL], NCH, ps_pool,
                    p4_toks, sqb, out6,
                    dst=lambda c: X4s[:, c * SL:(c + 1) * SL])
            slice_end = [p.token(e) for e in ENGINES]
        p.gap("gp", [p.token("sy")])
        p.finalize()
    return nc



def _shard_tokens(x):
    out = []
    for c in range(NCORE):
        b, pchunk = c // 4, c % 4
        sl = x[b, pchunk * TC:(pchunk + 1) * TC, :]
        out.append(np.ascontiguousarray(sl.T))
    return out


def _col128(v):
    n = v.shape[0] // 128
    return np.ascontiguousarray(v.reshape(n, 128).T)


def run_launch_A(inputs):
    nc = build_launch_A()
    xs = _shard_tokens(np.asarray(inputs["L0_attn_x"], dtype=np.float32))
    common = {
        "projT": np.ascontiguousarray(np.asarray(inputs["L0_attn_proj_w"], np.float32).T),
        "fc1T": np.ascontiguousarray(np.asarray(inputs["L0_attn_fc1_w"], np.float32).T),
        "fc2T": np.ascontiguousarray(np.asarray(inputs["L0_attn_fc2_w"], np.float32).T),
        "c_ln1_g": _col128(np.asarray(inputs["L0_attn_ln1_g"], np.float32)),
        "c_ln1_b": _col128(np.asarray(inputs["L0_attn_ln1_b"], np.float32)),
        "c_ln2_g": _col128(np.asarray(inputs["L0_attn_ln2_g"], np.float32)),
        "c_ln2_b": _col128(np.asarray(inputs["L0_attn_ln2_b"], np.float32)),
        "c_lnh_g": _col128(np.asarray(inputs["L0_attn_lnh_g"], np.float32)),
        "c_lnh_b": _col128(np.asarray(inputs["L0_attn_lnh_b"], np.float32)),
        "r_proj_b": np.asarray(inputs["L0_attn_proj_b"], np.float32)[None, :],
        "r_fc1_b": np.asarray(inputs["L0_attn_fc1_b"], np.float32)[None, :],
        "r_fc2_b": np.asarray(inputs["L0_attn_fc2_b"], np.float32)[None, :],
        "c_gamma1": _col128(np.asarray(inputs["L0_attn_gamma1"], np.float32)),
        "c_gamma2": _col128(np.asarray(inputs["L0_attn_gamma2"], np.float32)),
        "ident": np.eye(128, dtype=np.float32),
    }
    in_maps = [dict(common, xT=xs[c]) for c in range(NCORE)]
    res = run_bass_kernel_spmd(nc, in_maps, core_ids=list(range(NCORE)))
    X3 = [np.asarray(r["X3o"]) for r in res.results]
    max1 = [np.asarray(r["max1o"]) for r in res.results]
    return X3, max1


def run_launch_B(inputs, X3, max1):
    nc = build_launch_B()
    max1row = np.ascontiguousarray(
        np.stack([m.reshape(()) for m in max1])[None, :].astype(np.float32))
    common = {
        "max1r": max1row,
        "clipx": np.asarray(inputs["L0_ffn_clip_x"], np.float32).reshape(1, 1),
        "wg_nat": np.ascontiguousarray(np.asarray(inputs["L0_ffn_w_gate"], np.float32)),
        "wu_nat": np.ascontiguousarray(np.asarray(inputs["L0_ffn_w_up"], np.float32)),
        "ident": np.eye(128, dtype=np.float32),
    }
    in_maps = [dict(common, X3i=X3[c]) for c in range(NCORE)]
    res = run_bass_kernel_spmd(nc, in_maps, core_ids=list(range(NCORE)))
    hh = [np.asarray(r["hho"]) for r in res.results]
    max2 = [np.asarray(r["max2o"]) for r in res.results]
    if DEBUG_B or DEBUG_B_QX or DEBUG_B_SGUP:
        return hh, max2, res
    return hh, max2


def run_launch_C(inputs, X3, hh, max2):
    nc = build_launch_C()
    max2row = np.ascontiguousarray(
        np.stack([m.reshape(()) for m in max2])[None, :].astype(np.float32))
    common = {
        "max2r": max2row,
        "ident": np.eye(128, dtype=np.float32),
        "clipd": np.asarray(inputs["L0_ffn_clip_down"], np.float32).reshape(1, 1),
        "wd_nat": np.ascontiguousarray(np.asarray(inputs["L0_ffn_w_down"], np.float32)),
        "Wk": np.ascontiguousarray(np.asarray(inputs["L1_attn_Wk"], np.float32)),
        "Wv": np.ascontiguousarray(np.asarray(inputs["L1_attn_Wv"], np.float32)),
        "Wr": np.ascontiguousarray(np.asarray(inputs["L1_attn_Wr"], np.float32)),
        "c_tmk": _col128(np.asarray(inputs["L1_attn_time_mix_key"], np.float32)),
        "c_tmv": _col128(np.asarray(inputs["L1_attn_time_mix_value"], np.float32)),
        "c_tmr": _col128(np.asarray(inputs["L1_attn_time_mix_receptance"], np.float32)),
        "c_tdec": _col128(np.asarray(inputs["L1_attn_time_decay"], np.float32)),
    }
    in_maps = []
    _bcol = lambda v: np.ascontiguousarray(v.reshape(-1, 128).T)
    zcol_c = np.zeros((128, NCH), np.float32)
    zcol_i = np.zeros((128, NIF), np.float32)
    for c in range(NCORE):
        pp = c % 4
        if pp == 0:
            x3b, hhb = zcol_c, zcol_i
            bm = np.zeros((128, 1), np.float32)
        else:
            x3b = _bcol(X3[c - 1][:, TC - 1])
            hhb = _bcol(hh[c - 1][:, TC - 1])
            bm = np.ones((128, 1), np.float32)
        in_maps.append(dict(common, X3i=X3[c], X3b=x3b, hhi=hh[c], hhb=hhb,
                            bmask=bm))
    res = run_bass_kernel_spmd(nc, in_maps, core_ids=list(range(NCORE)))
    out = {}
    for key in ["X4o", "u_o", "e_o", "r_o", "summ_o"]:
        out[key] = [np.asarray(r[key]) for r in res.results]
    return out


def run_launch_D(inputs, Cout):
    nc = build_launch_D()
    G = 128
    common = {
        "Wo": np.ascontiguousarray(np.asarray(inputs["L1_attn_Wo"], np.float32)),
        "fc1q": np.ascontiguousarray(np.asarray(inputs["L1_ffn_fc1_q"], np.int32)),
        "fc1srep": np.ascontiguousarray(np.repeat(
            np.asarray(inputs["L1_ffn_fc1_scales"], np.float32), G, axis=0)),
        "fc2q": np.ascontiguousarray(np.asarray(inputs["L1_ffn_fc2_q"], np.int32)),
        "fc2srep": np.ascontiguousarray(np.repeat(
            np.asarray(inputs["L1_ffn_fc2_scales"], np.float32), G, axis=0)),
        "b1r": np.asarray(inputs["L1_ffn_fc1_bias"], np.float32)[None, :],
        "b2r": np.asarray(inputs["L1_ffn_fc2_bias"], np.float32)[None, :],
        "c_tdec": _col128(np.asarray(inputs["L1_attn_time_decay"], np.float32)),
        "c_tfirst": _col128(np.asarray(inputs["L1_attn_time_first"], np.float32)),
        "c_lng": _col128(np.asarray(inputs["L1_ffn_ln_gamma"], np.float32)),
        "c_lnb": _col128(np.asarray(inputs["L1_ffn_ln_beta"], np.float32)),
    }
    in_maps = []
    for c in range(NCORE):
        b, pp = c // 4, c % 4
        suma = np.concatenate(
            [Cout["summ_o"][4 * b + q][:, 0:NCH] for q in range(4)], axis=1)
        sumb = np.concatenate(
            [Cout["summ_o"][4 * b + q][:, NCH:2 * NCH] for q in range(4)], axis=1)
        m = np.array([max(0, TC * (pp - 1 - q)) for q in range(4)], np.float32)
        msk = np.array([1.0 if q < pp else 0.0 for q in range(4)], np.float32)
        in_maps.append(dict(
            common,
            X4i=Cout["X4o"][c], u_i=Cout["u_o"][c], e_i=Cout["e_o"][c],
            r_i=Cout["r_o"][c],
            sum_a=np.ascontiguousarray(suma, np.float32),
            sum_b=np.ascontiguousarray(sumb, np.float32),
            wkv_m=np.ascontiguousarray(np.broadcast_to(m, (128, 4))),
            wkv_mask=np.ascontiguousarray(np.broadcast_to(msk, (128, 4))),
        ))
    res = run_bass_kernel_spmd(nc, in_maps, core_ids=list(range(NCORE)))
    if DEBUG_D:
        return [{key: np.asarray(r[key]) for key in
                 ["X6o", "dbg_rw", "dbg_x5", "dbg_hf", "dbg_y", "dbg_p4",
                  "dbg_p3", "dbg_wo"]}
                for r in res.results]
    return [np.asarray(r["X6o"]) for r in res.results]


def kernel(**inputs):
    X3, max1 = run_launch_A(inputs)
    hh, max2 = run_launch_B(inputs, X3, max1)
    Cout = run_launch_C(inputs, X3, hh, max2)
    X6 = run_launch_D(inputs, Cout)
    out = np.empty((B, T, C), dtype=np.float32)
    for c in range(NCORE):
        b, pp = c // 4, c % 4
        out[b, pp * TC:(pp + 1) * TC, :] = X6[c].T
    return out
